# version 04
# baseline (speedup 1.0000x reference)
import contextlib
import numpy as np
import concourse.bass as bass
import concourse.mybir as mybir
from concourse.bass_utils import run_bass_kernel_spmd

F32 = mybir.dt.float32
BF16 = mybir.dt.bfloat16
A = mybir.AluOpType
AF = mybir.ActivationFunctionType
AX = mybir.AxisListType.X

B, S, H, I = 16, 2048, 1024, 2816
NCORES = 8
BPC = B // NCORES
TG = 512
GPS = S // TG
NGRP = BPC * GPS
NTB = TG // 128
KC = H // 128
ICN = I // 128
EPS = 1e-5

MIDS = (0.25, 0.75, 1.25, 1.75, 2.5, 3.5, 5.0)
WTS = (0.5, 0.5, 0.5, 0.5, 1.0, 1.0, 2.0)
NSL = [(0, 512), (512, 512), (1024, 512), (1536, 512), (2048, 512), (2560, 256)]

NEWTON_ITERS = 8


class Sem:
    __slots__ = ("name", "cnt", "h")

    def __init__(self, name):
        self.name = name
        self.cnt = 0
        self.h = None


class Sched:
    ENGS = ("sync", "gpsimd", "vector", "scalar", "tensor")

    def __init__(self):
        self.ev = {e: [] for e in self.ENGS}
        self.sems = []
        self.last_wait = {e: {} for e in self.ENGS}

    def sem(self, name):
        s = Sem(name)
        self.sems.append(s)
        return s

    def emit(self, eng, fn, waits=(), inc=None, inc_amt=1):
        real = []
        for w in waits:
            if w is None:
                continue
            s, v = w
            if v <= 0:
                continue
            if self.last_wait[eng].get(s, 0) < v:
                self.last_wait[eng][s] = v
                real.append((s, v))
        tok = None
        if inc is not None:
            inc.cnt += inc_amt
            tok = (inc, inc.cnt)
        self.ev[eng].append((fn, real, inc, inc_amt))
        return tok

    def replay(self, eng, e):
        for (fn, waits, inc, amt) in self.ev[eng]:
            for (s, v) in waits:
                e.wait_ge(s.h, v)
            inst = fn(e)
            if inc is not None:
                assert inst is not None, f"{eng}: inc without instruction"
                inst.then_inc(inc.h, amt)


def build_nc(debug=False):
    nc = bass.Bass("TRN2", target_bir_lowering=False)

    d_x = nc.dram_tensor("x", [BPC, S, H], F32, kind="ExternalInput").ap()
    d_xT = nc.dram_tensor("xT", [BPC, 128, KC, S], F32, kind="ExternalInput").ap()
    d_wvT = nc.dram_tensor("wvT", [H, H], F32, kind="ExternalInput").ap()
    d_wrT = nc.dram_tensor("wrT", [H, H], F32, kind="ExternalInput").ap()
    d_woT = nc.dram_tensor("woT", [H, H], F32, kind="ExternalInput").ap()
    d_wg = nc.dram_tensor("wg", [I, H], F32, kind="ExternalInput").ap()
    d_wu = nc.dram_tensor("wu", [I, H], F32, kind="ExternalInput").ap()
    d_wd = nc.dram_tensor("wd", [H, I], F32, kind="ExternalInput").ap()
    d_td = nc.dram_tensor("td", [H], F32, kind="ExternalInput").ap()
    d_tmv = nc.dram_tensor("tmv", [H], F32, kind="ExternalInput").ap()
    d_tmr = nc.dram_tensor("tmr", [H], F32, kind="ExternalInput").ap()
    d_ident = nc.dram_tensor("ident", [128, 128], F32, kind="ExternalInput").ap()
    d_out = nc.dram_tensor("out", [BPC, S, H], F32, kind="ExternalOutput").ap()

    sc_att = nc.dram_tensor("sc_att", [3, 128, KC, H], BF16, kind="Internal").ap()
    sc_wg = nc.dram_tensor("sc_wg", [128, KC, I], BF16, kind="Internal").ap()
    sc_wu = nc.dram_tensor("sc_wu", [128, KC, I], BF16, kind="Internal").ap()
    sc_wd = nc.dram_tensor("sc_wd", [128, ICN, H], BF16, kind="Internal").ap()

    dbg = {}
    if debug:
        for nm, shp in [
            ("dbg_y", [128, KC, TG]), ("dbg_vin", [128, KC, TG]),
            ("dbg_v", [128, KC, TG]), ("dbg_attn", [128, KC, TG]),
            ("dbg_x1", [128, NTB, H]), ("dbg_xq", [128, KC, TG]),
            ("dbg_h", [128, NTB, I]), ("dbg_dn", [128, KC, TG]),
            ("dbg_res", [128, NTB, H]), ("dbg_st", [128, NTB * 12]),
        ]:
            dbg[nm] = nc.dram_tensor(nm, shp, F32, kind="ExternalOutput").ap()
        dbg["dbg_wgq"] = nc.dram_tensor("dbg_wgq", [128, KC, I], BF16, kind="ExternalOutput").ap()
        dbg["dbg_gate"] = nc.dram_tensor("dbg_gate", [128, 2, I], F32, kind="ExternalOutput").ap()
        dbg["dbg_gsl"] = nc.dram_tensor("dbg_gsl", [128, KC, 512], BF16, kind="ExternalOutput").ap()
        dbg["dbg_hqf"] = nc.dram_tensor("dbg_hqf", [128, ICN, TG], BF16, kind="ExternalOutput").ap()
        dbg["dbg_xqf2"] = nc.dram_tensor("dbg_xqf2", [128, KC, TG], BF16, kind="ExternalOutput").ap()

    sch = Sched()
    sv = sch.sem("sv")
    ss = sch.sem("ss")
    sp = sch.sem("sp")
    dg = sch.sem("dg")
    ds = sch.sem("ds")
    dsc = sch.sem("dsc")
    sg = sch.sem("sg")
    E = sch.emit

    ctx = contextlib.ExitStack()

    def sb(name, shape, dt):
        return ctx.enter_context(nc.sbuf_tensor(name, shape, dt))

    def pst(name, shape, dt):
        return ctx.enter_context(nc.psum_tensor(name, shape, dt))

    with ctx:
        b_d = sb("b_d", [128, KC], F32)
        b_cy = sb("b_cy", [128, KC], F32)
        b_tmv = sb("b_tmv", [128, KC], F32)
        b_tmr = sb("b_tmr", [128, KC], F32)
        b_idb = sb("b_idb", [128, 128], BF16)
        b_wp = [sb(f"b_wp{i}", [128, KC, 512], BF16) for i in range(2)]
        b_xf = sb("b_xf", [128, KC, TG], BF16)
        b_y = sb("b_y", [128, KC, TG + 2], BF16)
        b_vin2 = [sb(f"b_vin{i}", [128, KC, TG], BF16) for i in range(2)]
        b_rin2 = [sb(f"b_rin{i}", [128, KC, TG], BF16) for i in range(2)]
        b_vsb = sb("b_vsb", [128, KC, TG], BF16)
        b_rs = sb("b_rs", [128, KC, TG], BF16)
        b_xtm = sb("b_xtm", [128, 2, H], F32)
        b_x1 = sb("b_x1", [128, NTB, H], F32)
        b_abs = sb("b_abs", [128, H], F32)
        b_w = sb("b_w", [128, H], F32)
        b_sgn = sb("b_sgn", [128, H], BF16)
        b_mA = sb("b_mA", [128, H], BF16)
        b_mB = sb("b_mB", [128, H], BF16)
        b_mm = sb("b_mm", [128, H], BF16)
        b_bm = sb("b_bm", [128, 176], F32)
        b_sc = sb("b_sc", [128, 176], F32)
        b_rc = sb("b_rc", [128, 176], F32)
        b_sb16 = sb("b_sb16", [128, 176], BF16)
        b_st = sb("b_st", [128, NTB * 12], F32)
        b_xq = sb("b_xq", [128, H], BF16)
        b_xqf = sb("b_xqf", [128, KC, TG], BF16)
        b_gsl = [sb(f"b_gsl{i}", [128, KC, 512], BF16) for i in range(2)]
        b_usl = [sb(f"b_usl{i}", [128, KC, 512], BF16) for i in range(2)]
        b_dsl = [sb(f"b_dsl{i}", [128, ICN, 128], BF16) for i in range(2)]
        b_h = sb("b_h", [128, 512], F32)
        b_hq = sb("b_hq", [128, 512], BF16)
        b_gf = sb("b_gf", [128, 512], F32)
        b_uf = sb("b_uf", [128, 512], F32)
        b_hqf = sb("b_hqf", [128, ICN, TG], BF16)

        b_dn = b_gsl[0][:].rearrange("p kc n -> p kc n") if False else b_gsl[0]
        b_wnat = b_x1[:].rearrange("p tb h -> p (tb h)")[:, :I]
        b_sq = b_w[:]
        b_stage = b_xqf[:, :, 0:128]
        b_sig = b_abs[:, 0:512]
        b_t = b_abs[:, 512:1024]

        pA = pst("pA", [128, 512], F32)
        pB = pst("pB", [128, 512], F32)
        pC = pst("pC", [128, 512], F32)
        pD = pst("pD", [128, 512], F32)
        pG = pst("pG", [128, 512], F32)
        pE = pst("pE", [128, 1024], BF16)
        pF = pst("pF", [128, 1024], BF16)
        pF2 = pst("pF2", [128, 1024], BF16)

        sem_ctx = contextlib.ExitStack()
        with sem_ctx:
            for s_ in sch.sems:
                s_.h = sem_ctx.enter_context(nc.semaphore(s_.name))
            blk = ctx.enter_context(nc.Block())

            free = {}
            qd_war = [None]

            def bank_wait(p):
                return free.get(id(p))

            def qd_tile(src_ap, dst_ap, fd, wait_src, dst_war=None):
                W = fd // 16
                sv3 = src_ap.rearrange("p (w k) -> p w k", k=16)
                war = qd_war[0]
                tv = E("vector", lambda e: e.tensor_reduce(
                    b_bm[:, :W], sv3, AX, A.max, apply_absolute_value=True),
                    waits=[wait_src, war])
                ts = E("scalar", lambda e: e.activation(b_abs[:, :fd], src_ap, AF.Abs),
                       waits=[wait_src, war], inc=ss)
                ts2 = E("scalar", lambda e: e.activation(b_sgn[:, :fd], src_ap, AF.Sign),
                        inc=ss)
                E("vector", lambda e: e.tensor_copy(b_st[:, 8:40], b_st[:, 8:40]))
                E("vector", lambda e: e.tensor_scalar(
                    b_sc[:, :W], b_bm[:, :W], 1.0 / 6.0, 1e-12, A.mult, A.max))
                E("vector", lambda e: e.tensor_copy(b_st[:, 8:40], b_st[:, 8:40]))
                E("vector", lambda e: e.reciprocal(b_rc[:, :W], b_sc[:, :W]))
                E("vector", lambda e: e.tensor_copy(b_sb16[:, :W], b_sc[:, :W]))
                r_b = b_rc[:, :W].rearrange("p (w o) -> p w o", o=1).broadcast_to([128, W, 16])
                s_b = b_sb16[:, :W].rearrange("p (w o) -> p w o", o=1).broadcast_to([128, W, 16])
                wv3 = b_w[:, :fd].rearrange("p (w k) -> p w k", k=16)
                av3 = b_abs[:, :fd].rearrange("p (w k) -> p w k", k=16)
                E("vector", lambda e: e.tensor_tensor(wv3, av3, r_b, A.mult), waits=[ts])
                wap = b_w[:, :fd]
                E("vector", lambda e: e.tensor_scalar(b_mA[:, :fd], wap, MIDS[0], WTS[0], A.is_gt, A.mult))
                for k in range(1, 7):
                    E("vector", lambda e, k=k: e.tensor_scalar(b_mB[:, :fd], wap, MIDS[k], WTS[k], A.is_gt, A.mult))
                    E("vector", lambda e: e.tensor_tensor(b_mA[:, :fd], b_mA[:, :fd], b_mB[:, :fd], A.add))
                t_masks = E("vector", lambda e: e.tensor_copy(
                    b_st[:, 40:44], b_st[:, 40:44]), inc=sv)
                mm3 = b_mm[:, :fd].rearrange("p (w k) -> p w k", k=16)
                g3 = b_sgn[:, :fd].rearrange("p (w k) -> p w k", k=16)
                E("gpsimd", lambda e: e.tensor_tensor(mm3, g3, s_b, A.mult),
                  waits=[ts2, t_masks])
                tq = E("gpsimd", lambda e: e.tensor_tensor(
                    dst_ap, b_mA[:, :fd], b_mm[:, :fd], A.mult),
                    waits=[dst_war], inc=sg)
                qd_war[0] = tq
                return tq

            def ln_stats(src_full, tb, waits_v):
                c = tb * 12
                E("vector", lambda e: e.tensor_reduce(
                    b_st[:, c:c + 1], src_full, AX, A.add), waits=waits_v)
                E("vector", lambda e: e.scalar_tensor_tensor(
                    b_sq, src_full, 1.0, src_full, A.mult, A.mult,
                    accum_out=b_st[:, c + 1:c + 2]))

            def ln_batch(dst_srcs, waits_dst):
                nt = len(dst_srcs)

                def step(fn):
                    for tb in range(nt):
                        fn(tb, tb * 12)
                step(lambda tb, c: E("vector", lambda e: e.tensor_single_scalar(
                    b_st[:, c + 2:c + 3], b_st[:, c:c + 1], 1.0 / H, A.mult)))
                step(lambda tb, c: E("vector", lambda e: e.tensor_single_scalar(
                    b_st[:, c + 3:c + 4], b_st[:, c + 1:c + 2], 1.0 / H, A.mult)))
                step(lambda tb, c: E("vector", lambda e: e.tensor_tensor(
                    b_st[:, c + 4:c + 5], b_st[:, c + 2:c + 3], b_st[:, c + 2:c + 3], A.mult)))
                step(lambda tb, c: E("vector", lambda e: e.tensor_tensor(
                    b_st[:, c + 3:c + 4], b_st[:, c + 3:c + 4], b_st[:, c + 4:c + 5], A.subtract)))
                step(lambda tb, c: E("vector", lambda e: e.tensor_single_scalar(
                    b_st[:, c + 3:c + 4], b_st[:, c + 3:c + 4], EPS, A.add)))
                step(lambda tb, c: E("vector", lambda e: e.reciprocal(
                    b_st[:, c + 5:c + 6], b_st[:, c + 3:c + 4])))
                step(lambda tb, c: E("vector", lambda e: e.tensor_single_scalar(
                    b_st[:, c + 5:c + 6], b_st[:, c + 5:c + 6], 1.0, A.min)))
                for _ in range(NEWTON_ITERS):
                    step(lambda tb, c: E("vector", lambda e: e.tensor_tensor(
                        b_st[:, c + 6:c + 7], b_st[:, c + 5:c + 6], b_st[:, c + 5:c + 6], A.mult)))
                    step(lambda tb, c: E("vector", lambda e: e.tensor_tensor(
                        b_st[:, c + 6:c + 7], b_st[:, c + 6:c + 7], b_st[:, c + 3:c + 4], A.mult)))
                    step(lambda tb, c: E("vector", lambda e: e.tensor_scalar(
                        b_st[:, c + 6:c + 7], b_st[:, c + 6:c + 7], -0.5, 1.5, A.mult, A.add)))
                    step(lambda tb, c: E("vector", lambda e: e.tensor_tensor(
                        b_st[:, c + 5:c + 6], b_st[:, c + 5:c + 6], b_st[:, c + 6:c + 7], A.mult)))
                step(lambda tb, c: E("vector", lambda e: e.tensor_tensor(
                    b_st[:, c + 7:c + 8], b_st[:, c + 2:c + 3], b_st[:, c + 5:c + 6], A.mult)))
                tns = []
                for tb in range(nt):
                    c = tb * 12
                    tns.append(E("vector", lambda e, c=c: e.tensor_single_scalar(
                        b_st[:, c + 7:c + 8], b_st[:, c + 7:c + 8], -1.0, A.mult), inc=sv))
                outs = []
                for tb in range(nt):
                    c = tb * 12
                    w_extra = waits_dst[tb] if waits_dst else None
                    outs.append(E("scalar", lambda e, tb=tb, c=c: e.activation(
                        dst_srcs[tb], dst_srcs[tb], AF.Identity,
                        bias=b_st[:, c + 7:c + 8], scale=b_st[:, c + 5:c + 6]),
                        waits=[tns[tb], w_extra], inc=ss))
                return outs

            t_d = E("sync", lambda e: e.dma_start(
                out=b_d[:], in_=d_td.rearrange("(kc p) -> p kc", p=128),
                allow_slow_non_contiguous=True), inc=ds, inc_amt=16)
            t_tmv = E("sync", lambda e: e.dma_start(
                out=b_tmv[:], in_=d_tmv.rearrange("(kc p) -> p kc", p=128),
                allow_slow_non_contiguous=True), inc=ds, inc_amt=16)
            t_tmr = E("sync", lambda e: e.dma_start(
                out=b_tmr[:], in_=d_tmr.rearrange("(kc p) -> p kc", p=128),
                allow_slow_non_contiguous=True), inc=ds, inc_amt=16)
            t_id = E("sync", lambda e: e.dma_start(out=b_bm[:, :128].rearrange("p (a b) -> p a b", b=128) if False else b_bm[:, :128], in_=d_ident[:]),
                     inc=ds, inc_amt=16)
            E("scalar", lambda e: e.activation(b_d[:], b_d[:], AF.Exp), waits=[t_d])
            E("scalar", lambda e: e.activation(b_tmv[:], b_tmv[:], AF.Identity),
              waits=[t_tmv])
            t_dd = E("scalar", lambda e: e.activation(b_d[:], b_d[:], AF.Exp, scale=-1.0),
                     inc=ss)
            t_idb = E("vector", lambda e: e.tensor_copy(b_idb[:], b_bm[:, :128]),
                      waits=[t_id], inc=sv)

            t_att = []
            bounce_war = [None, None]
            for pi, dw in enumerate((d_wvT, d_wrT, d_woT)):
                last = None
                for kc in range(KC):
                    bi = (pi * KC + kc) % 2
                    tl = E("gpsimd", lambda e, dw=dw, kc=kc, bi=bi: e.dma_start(
                        out=b_wp[bi][:].rearrange("p kc m -> p (kc m)")[:, :H],
                        in_=dw[kc * 128:(kc + 1) * 128, :]),
                        waits=[bounce_war[bi]], inc=dg, inc_amt=16)
                    last = E("sync", lambda e, pi=pi, kc=kc, bi=bi: e.dma_start(
                        out=sc_att[pi, :, kc, :],
                        in_=b_wp[bi][:].rearrange("p kc m -> p (kc m)")[:, :H]),
                        waits=[tl], inc=ds, inc_amt=16)
                    bounce_war[bi] = last
                t_att.append(last)
            wp_war = [bounce_war[0], bounce_war[1]]

            pf_war = [None]
            stage_war = None
            wnat_war = None
            for dsrc, dscr in ((d_wg, sc_wg), (d_wu, sc_wu)):
                for ch in range(ICN):
                    tl = E("sync", lambda e, dsrc=dsrc, ch=ch: e.dma_start(
                        out=b_wnat[:, :H], in_=dsrc[ch * 128:(ch + 1) * 128, :]),
                        waits=[wnat_war], inc=ds, inc_amt=16)
                    tq = qd_tile(b_wnat[:, :H], b_xq[:], H, tl, dst_war=pf_war[0])
                    wnat_war = tq
                    tp = None
                    for hc in range(KC):
                        tp = E("tensor", lambda e, hc=hc: e.transpose(
                            pF[:, hc * 128:(hc + 1) * 128],
                            b_xq[:, hc * 128:(hc + 1) * 128], b_idb[:]),
                            waits=[tq, t_idb, bank_wait(pF)],
                            inc=sp if hc == KC - 1 else None)
                    te = E("scalar", lambda e: e.copy(
                        b_stage, pF[:].rearrange("p (kc n) -> p kc n", n=128)),
                        waits=[tp, stage_war], inc=ss)
                    free[id(pF)] = te
                    pf_war[0] = tp
                    stage_war = E("sync", lambda e, dscr=dscr, ch=ch: e.dma_start(
                        out=dscr[:, :, ch * 128:(ch + 1) * 128], in_=b_stage),
                        waits=[te], inc=ds, inc_amt=16)
                    if debug and dscr is sc_wg:
                        stage_war = E("gpsimd", lambda e, ch=ch: e.dma_start(
                            out=dbg["dbg_wgq"][:, :, ch * 128:(ch + 1) * 128],
                            in_=b_stage), waits=[te], inc=dg, inc_amt=16)
            xq_war_seed = pf_war[0]
            dn_prep_war = None
            for ch in range(KC):
                tl = E("sync", lambda e, ch=ch: e.dma_start(
                    out=b_wnat[:, :I], in_=d_wd[ch * 128:(ch + 1) * 128, :]),
                    waits=[wnat_war], inc=ds, inc_amt=16)
                dq = b_dn[:].rearrange("p kc t -> p (kc t)")
                piece = []
                for oi, off in enumerate((0, 1024, 2048)):
                    fd = min(1024, I - off)
                    piece.append(qd_tile(b_wnat[:, off:off + fd], dq[:, off:off + fd],
                                         fd, tl, dst_war=dn_prep_war))
                wnat_war = piece[-1]
                for r0 in range(0, ICN, 8):
                    nt = min(8, ICN - r0)
                    tp = None
                    for j in range(nt):
                        ic = r0 + j
                        tp = E("tensor", lambda e, ic=ic, j=j: e.transpose(
                            pF[:, j * 128:(j + 1) * 128],
                            dq[:, ic * 128:(ic + 1) * 128], b_idb[:]),
                            waits=piece + [t_idb, bank_wait(pF)],
                            inc=sp if j == nt - 1 else None)
                    dn_prep_war = tp
                    te = E("scalar", lambda e, nt=nt: e.copy(
                        b_stage[:, :nt, :], pF[:, :nt * 128].rearrange("p (kc n) -> p kc n", n=128)),
                        waits=[tp, stage_war], inc=ss)
                    free[id(pF)] = te
                    stage_war = E("sync", lambda e, ch=ch, r0=r0, nt=nt: e.dma_start(
                        out=sc_wd[:, r0:r0 + nt, ch * 128:(ch + 1) * 128],
                        in_=b_stage[:, :nt, :]),
                        waits=[te], inc=ds, inc_amt=16)
            t_prep_done = (ds, ds.cnt)
            wnat_guard = wnat_war
            stage_guard = stage_war

            E("vector", lambda e: e.memzero(b_y[:, :, 0:2]))
            t_carry = E("vector", lambda e: e.memzero(b_cy[:]), inc=sv)

            St = {
                "prev_mix": None, "prev_out": [], "war_ab": [None, None],
                "xq_war": xq_war_seed, "gsl_war": [None, None],
                "usl_war": [None, None], "dsl_war": [None, None],
                "t_carry": t_carry, "t_xf": None, "xtm_war": [None, None],
                "mix": [None] * NGRP, "attn_toks": {}, "evac_v": {}, "evac_r": {},
                "x1_toks": {}, "xqf_toks": {}, "hq_toks": {}, "dn_toks": {},
                "first_out": True,
            }

            def emit_AB(g):
                seq = g // GPS
                t0 = (g % GPS) * TG
                p = g % 2
                b_vin, b_rin = b_vin2[p], b_rin2[p]
                t_xf = E("gpsimd", lambda e, seq=seq, t0=t0: e.dma_start(
                    out=b_xf[:], in_=d_xT[seq, :, :, t0:t0 + TG]),
                    waits=[St["prev_mix"]], inc=dg, inc_amt=16)
                if g > 0:
                    if g % GPS == 0:
                        E("vector", lambda e: e.memzero(b_y[:, :, 0:2]))
                        St["t_carry"] = E("vector", lambda e: e.memzero(b_cy[:]), inc=sv)
                    else:
                        E("vector", lambda e: e.tensor_copy(
                            b_y[:, :, 0:1], b_y[:, :, TG:TG + 1]))
                        E("vector", lambda e: e.tensor_copy(
                            b_cy[:].rearrange("p (kc o) -> p kc o", o=1),
                            b_y[:, :, TG:TG + 1]))
                        St["t_carry"] = E("vector", lambda e: e.tensor_copy(
                            b_st[:, 8:40], b_st[:, 8:40]), inc=sv)
                for kc in range(KC):
                    d_bb = b_d[:, kc:kc + 1].broadcast_to([128, TG])
                    E("vector", lambda e, kc=kc, d_bb=d_bb: e.tensor_tensor_scan(
                        b_y[:, kc, 1:TG + 1], d_bb, b_xf[:, kc, :],
                        b_cy[:, kc:kc + 1], A.mult, A.add),
                        waits=[t_xf, St["t_carry"], t_dd])
                war = St["war_ab"][p]
                for kc in range(KC):
                    E("vector", lambda e, kc=kc, b_rin=b_rin: e.tensor_tensor(
                        b_rin[:, kc, :], b_xf[:, kc, :], b_y[:, kc, 0:TG], A.subtract),
                        waits=[war])
                    E("vector", lambda e, kc=kc, b_vin=b_vin, b_rin=b_rin: e.scalar_tensor_tensor(
                        b_vin[:, kc, :], b_rin[:, kc, :], b_tmv[:, kc:kc + 1],
                        b_y[:, kc, 0:TG], A.mult, A.add), waits=[t_tmv])
                    E("vector", lambda e, kc=kc, b_rin=b_rin: e.scalar_tensor_tensor(
                        b_rin[:, kc, :], b_rin[:, kc, :], b_tmr[:, kc:kc + 1],
                        b_y[:, kc, 0:TG], A.mult, A.add), waits=[t_tmr])
                mix = E("vector", lambda e: e.tensor_copy(b_cy[:, 0:1], b_cy[:, 0:1]), inc=sv)
                St["prev_mix"] = mix
                St["mix"][g] = mix
                if debug and g == 0:
                    E("gpsimd", lambda e: e.dma_start(out=dbg["dbg_y"], in_=b_y[:, :, 1:TG + 1]),
                      waits=[mix], inc=dg, inc_amt=16)
                    E("gpsimd", lambda e, b_vin=b_vin: e.dma_start(out=dbg["dbg_vin"], in_=b_vin[:]),
                      waits=[mix], inc=dg, inc_amt=16)

            def emit_C(g):
                p = g % 2
                b_vin, b_rin = b_vin2[p], b_rin2[p]
                b_attn = b_vin
                mix = St["mix"][g]
                banks = [pA, pB]
                evac_v = []
                evac_r = []
                attn_toks = []
                for item in range(6):
                    proj, half = divmod(item, 2)
                    bi = item % 2
                    tload = E("sync", lambda e, proj=proj, half=half, bi=bi: e.dma_start(
                        out=b_wp[bi][:],
                        in_=sc_att[proj, :, :, half * 512:(half + 1) * 512]),
                        waits=[t_att[proj], wp_war[bi]], inc=ds, inc_amt=16)
                    last_mm = None
                    for mc4 in range(4):
                        mc = half * 4 + mc4
                        bank = banks[mc % 2]
                        tmm = None
                        for kc in range(KC):
                            if proj < 2:
                                rhs = b_vin if proj == 0 else b_rin
                                tmm = E("tensor", lambda e, bi=bi, kc=kc, mc4=mc4, bank=bank, rhs=rhs: e.matmul(
                                    bank[:], b_wp[bi][:, kc, mc4 * 128:(mc4 + 1) * 128],
                                    rhs[:, kc, :], start=(kc == 0), stop=(kc == KC - 1)),
                                    waits=[mix, tload, bank_wait(bank)],
                                    inc=sp if kc == KC - 1 else None)
                            else:
                                tmm = E("tensor", lambda e, bi=bi, kc=kc, mc4=mc4, bank=bank: e.matmul(
                                    bank[:], b_wp[bi][:, kc, mc4 * 128:(mc4 + 1) * 128],
                                    b_vsb[:, kc, :], start=(kc == 0), stop=(kc == KC - 1)),
                                    waits=[tload, bank_wait(bank)] + evac_v,
                                    inc=sp if kc == KC - 1 else None)
                        last_mm = tmm
                        if proj == 0:
                            te = E("scalar", lambda e, mc=mc, bank=bank: e.copy(
                                b_vsb[:, mc, :], bank[:]), waits=[tmm], inc=ss)
                            evac_v.append(te)
                            free[id(bank)] = te
                        elif proj == 1:
                            te = E("scalar", lambda e, mc=mc, bank=bank: e.activation(
                                b_rs[:, mc, :], bank[:], AF.Sigmoid), waits=[tmm], inc=ss)
                            evac_r.append(te)
                            free[id(bank)] = te
                        else:
                            ta = E("vector", lambda e, mc=mc, bank=bank, b_attn=b_attn: e.tensor_tensor(
                                b_attn[:, mc, :], b_rs[:, mc, :], bank[:], A.mult),
                                waits=[tmm, evac_r[mc]], inc=sv)
                            attn_toks.append(ta)
                            free[id(bank)] = ta
                    wp_war[bi] = last_mm
                St["attn_toks"][g] = attn_toks
                St["evac_v"][g] = evac_v
                if debug and g == 0:
                    E("gpsimd", lambda e: e.dma_start(out=dbg["dbg_v"], in_=b_vsb[:]),
                      waits=evac_v, inc=dg, inc_amt=16)
                    E("gpsimd", lambda e, b_attn=b_attn: e.dma_start(out=dbg["dbg_attn"], in_=b_attn[:]),
                      waits=attn_toks, inc=dg, inc_amt=16)

            def emit_D(g):
                p = g % 2
                seq = g // GPS
                gtb0 = (g % GPS) * NTB
                b_attn = b_vin2[p]
                attn_toks = St["attn_toks"][g]
                prev_out = St["prev_out"]
                res_toks = []
                tp = None

                def xtm_load(tb):
                    return E("sync", lambda e, seq=seq, gtb0=gtb0, tb=tb: e.dma_start(
                        out=b_xtm[:, tb % 2, :],
                        in_=d_x.rearrange("s (tb p) h -> s p tb h", p=128)[seq, :, gtb0 + tb, :]),
                        waits=[St["xtm_war"][tb % 2]], inc=ds, inc_amt=16)
                xtm_toks = [xtm_load(0), xtm_load(1), None, None]
                for tb in range(NTB):
                    for oc in range(KC):
                        tp = E("tensor", lambda e, oc=oc, tb=tb, b_attn=b_attn: e.transpose(
                            pE[:, oc * 128:(oc + 1) * 128],
                            b_attn[:, oc, tb * 128:(tb + 1) * 128], b_idb[:]),
                            waits=[attn_toks[oc], t_idb, bank_wait(pE)],
                            inc=sp if oc == KC - 1 else None)
                    tr = E("vector", lambda e, tb=tb: e.tensor_tensor(
                        b_x1[:, tb, :], b_xtm[:, tb % 2, :], pE[:], A.add),
                        waits=[tp, xtm_toks[tb],
                               (prev_out[tb] if prev_out else wnat_guard)], inc=sv)
                    free[id(pE)] = tr
                    St["xtm_war"][tb % 2] = tr
                    if tb + 2 < NTB:
                        xtm_toks[tb + 2] = xtm_load(tb + 2)
                    res_toks.append(tr)
                    ln_stats(b_x1[:, tb, :], tb, [])
                    if debug and g == 0:
                        tdmp = E("gpsimd", lambda e, tb=tb: e.dma_start(
                            out=dbg["dbg_res"][:, tb, :], in_=b_x1[:, tb, :]),
                            waits=[tr], inc=dg, inc_amt=16)
                        res_toks[-1] = tdmp
                St["war_ab"][p] = tp
                x1_toks = ln_batch([b_x1[:, tb, :] for tb in range(NTB)],
                                   res_toks if debug else None)
                St["x1_toks"][g] = x1_toks
                if debug and g == 0:
                    E("gpsimd", lambda e: e.dma_start(out=dbg["dbg_x1"], in_=b_x1[:]),
                      waits=x1_toks, inc=dg, inc_amt=16)
                    E("gpsimd", lambda e: e.dma_start(out=dbg["dbg_st"], in_=b_st[:]),
                      waits=x1_toks, inc=dg, inc_amt=16)

            def emit_E(g):
                xqf_toks = []
                x1_toks = St["x1_toks"][g]
                for tb in range(NTB):
                    tq = qd_tile(b_x1[:, tb, :], b_xq[:], H, x1_toks[tb],
                                 dst_war=St["xq_war"])
                    tp = None
                    for kc in range(KC):
                        tp = E("tensor", lambda e, kc=kc: e.transpose(
                            pF[:, kc * 128:(kc + 1) * 128],
                            b_xq[:, kc * 128:(kc + 1) * 128], b_idb[:]),
                            waits=[tq, t_idb, bank_wait(pF)],
                            inc=sp if kc == KC - 1 else None)
                    St["xq_war"] = tp
                    te = E("scalar", lambda e, tb=tb: e.copy(
                        b_xqf[:, :, tb * 128:(tb + 1) * 128],
                        pF[:].rearrange("p (kc t) -> p kc t", t=128)),
                        waits=[tp, stage_guard if g == 0 and tb == 0 else None], inc=ss)
                    free[id(pF)] = te
                    xqf_toks.append(te)
                St["xqf_toks"][g] = xqf_toks
                if debug and g == 0:
                    E("gpsimd", lambda e: e.dma_start(out=dbg["dbg_xq"], in_=b_xqf[:]),
                      waits=xqf_toks, inc=dg, inc_amt=16)

            def emit_F(g):
                hq_toks = []
                xqf_toks = St["xqf_toks"][g]
                fbanks = [(pA, pB), (pC, pD)]
                prev_sig = None
                hq_dst_war = [None]
                gsl_war = St["gsl_war"]
                usl_war = St["usl_war"]
                for si, (n0, ns) in enumerate(NSL):
                    tgd = E("sync", lambda e, si=si, n0=n0, ns=ns: e.dma_start(
                        out=b_gsl[si % 2][:, :, :ns],
                        in_=sc_wg[:, :, n0:n0 + ns]),
                        waits=[t_prep_done if g == 0 else None, gsl_war[si % 2]],
                        inc=ds, inc_amt=16)
                    tud = E("sync", lambda e, si=si, n0=n0, ns=ns: e.dma_start(
                        out=b_usl[si % 2][:, :, :ns],
                        in_=sc_wu[:, :, n0:n0 + ns]),
                        waits=[usl_war[si % 2]], inc=ds, inc_amt=16)
                    if debug and g == 0 and si == 0:
                        E("gpsimd", lambda e: e.dma_start(out=dbg["dbg_gsl"], in_=b_gsl[0][:]),
                          waits=[tgd], inc=dg, inc_amt=16)
                        E("gpsimd", lambda e: e.dma_start(out=dbg["dbg_xqf2"], in_=b_xqf[:]),
                          waits=[xqf_toks[3]], inc=dg, inc_amt=16)
                    for tb in range(NTB):
                        bg, bu = fbanks[(si * NTB + tb) % 2]
                        tmmg = tmmu = None
                        for kc in range(KC):
                            tmmg = E("tensor", lambda e, kc=kc, tb=tb, si=si, ns=ns, bg=bg: e.matmul(
                                bg[:, :ns], b_xqf[:, kc, tb * 128:(tb + 1) * 128],
                                b_gsl[si % 2][:, kc, :ns],
                                start=(kc == 0), stop=(kc == KC - 1)),
                                waits=[xqf_toks[tb], tgd, bank_wait(bg)],
                                inc=sp if kc == KC - 1 else None)
                            tmmu = E("tensor", lambda e, kc=kc, tb=tb, si=si, ns=ns, bu=bu: e.matmul(
                                bu[:, :ns], b_xqf[:, kc, tb * 128:(tb + 1) * 128],
                                b_usl[si % 2][:, kc, :ns],
                                start=(kc == 0), stop=(kc == KC - 1)),
                                waits=[tud, bank_wait(bu)],
                                inc=sp if kc == KC - 1 else None)
                        if tb == NTB - 1:
                            gsl_war[si % 2] = tmmg
                            usl_war[si % 2] = tmmu
                        if debug and g == 0 and tb == 0:
                            tcg = E("scalar", lambda e, ns=ns, bg=bg: e.copy(
                                b_sq[:, :ns], bg[:, :ns]), waits=[tmmg, prev_sig], inc=ss)
                            tdg = E("gpsimd", lambda e, n0=n0, ns=ns: e.dma_start(
                                out=dbg["dbg_gate"][:, 0, n0:n0 + ns], in_=b_sq[:, :ns]),
                                waits=[tcg], inc=dg, inc_amt=16)
                            tcu = E("scalar", lambda e, ns=ns, bu=bu: e.copy(
                                b_sq[:, :ns], bu[:, :ns]), waits=[tmmu, tdg], inc=ss)
                            tdu = E("gpsimd", lambda e, n0=n0, ns=ns: e.dma_start(
                                out=dbg["dbg_gate"][:, 1, n0:n0 + ns], in_=b_sq[:, :ns]),
                                waits=[tcu], inc=dg, inc_amt=16)
                            prev_sig = tdu
                        tsg = E("scalar", lambda e, ns=ns, bg=bg: e.activation(
                            b_sig[:, :ns], bg[:, :ns], AF.Sigmoid),
                            waits=[tmmg, prev_sig], inc=ss)
                        teg = E("scalar", lambda e, ns=ns, bg=bg: e.copy(
                            b_gf[:, :ns], bg[:, :ns]), inc=ss)
                        free[id(bg)] = teg
                        teu = E("scalar", lambda e, ns=ns, bu=bu: e.copy(
                            b_uf[:, :ns], bu[:, :ns]), waits=[tmmu], inc=ss)
                        free[id(bu)] = teu
                        tt = E("vector", lambda e, ns=ns: e.tensor_tensor(
                            b_t[:, :ns], b_sig[:, :ns], b_uf[:, :ns], A.mult),
                            waits=[teu], inc=sv)
                        th = E("vector", lambda e, ns=ns: e.tensor_tensor(
                            b_h[:, :ns], b_t[:, :ns], b_gf[:, :ns], A.mult),
                            waits=[teg], inc=sv)
                        tqh = qd_tile(b_h[:, :ns], b_hq[:, :ns], ns, th,
                                      dst_war=hq_dst_war[0])
                        prev_sig = tqh
                        ntl = ns // 128
                        tp = None
                        for j in range(ntl):
                            tp = E("tensor", lambda e, j=j: e.transpose(
                                pF[:, j * 128:(j + 1) * 128],
                                b_hq[:, j * 128:(j + 1) * 128], b_idb[:]),
                                waits=[tqh, t_idb, bank_wait(pF)],
                                inc=sp if j == ntl - 1 else None)
                        hq_dst_war[0] = tp
                        te = E("scalar", lambda e, tb=tb, n0=n0, ntl=ntl: e.copy(
                            b_hqf[:, n0 // 128:n0 // 128 + ntl, tb * 128:(tb + 1) * 128],
                            pF[:, :ntl * 128].rearrange("p (ic t) -> p ic t", t=128)),
                            waits=[tp], inc=ss)
                        free[id(pF)] = te
                        hq_toks.append(te)
                        if debug and g == 0:
                            E("gpsimd", lambda e, tb=tb, n0=n0, ns=ns: e.dma_start(
                                out=dbg["dbg_h"][:, tb, n0:n0 + ns], in_=b_h[:, :ns]),
                                waits=[tqh], inc=dg, inc_amt=16)
                St["hq_toks"][g] = hq_toks

            def emit_G(g):
                seq = g // GPS
                t0 = (g % GPS) * TG
                hq_toks = St["hq_toks"][g]
                dsl_war = St["dsl_war"]
                hq_dump_tok = None
                if debug and g == 0:
                    hq_dump_tok = E("gpsimd", lambda e: e.dma_start(
                        out=dbg["dbg_hqf"], in_=b_hqf[:]), waits=hq_toks, inc=dg, inc_amt=16)
                dn_toks = []
                dbanks = [pG, pA]
                for oc in range(KC):
                    tdw = E("sync", lambda e, oc=oc: e.dma_start(
                        out=b_dsl[oc % 2][:],
                        in_=sc_wd[:, :, oc * 128:(oc + 1) * 128]),
                        waits=[t_prep_done if g == 0 else None, dsl_war[oc % 2]],
                        inc=ds, inc_amt=16)
                    bank = dbanks[oc % 2]
                    tmm = None
                    for ic in range(ICN):
                        tmm = E("tensor", lambda e, ic=ic, oc=oc, bank=bank: e.matmul(
                            bank[:], b_dsl[oc % 2][:, ic, :], b_hqf[:, ic, :],
                            start=(ic == 0), stop=(ic == ICN - 1)),
                            waits=hq_toks + [tdw, bank_wait(bank), hq_dump_tok],
                            inc=sp if ic == ICN - 1 else None)
                    dsl_war[oc % 2] = tmm
                    te = E("scalar", lambda e, oc=oc, bank=bank: e.copy(
                        b_dn[:, oc, :], bank[:]), waits=[tmm], inc=ss)
                    free[id(bank)] = te
                    dn_toks.append(te)
                if debug and g == 0:
                    E("gpsimd", lambda e: e.dma_start(out=dbg["dbg_dn"], in_=b_dn[:]),
                      waits=dn_toks, inc=dg, inc_amt=16)
                out_toks = []
                res2_toks = []
                for tb in range(NTB):
                    tp = None
                    for oc in range(KC):
                        tp = E("tensor", lambda e, oc=oc, tb=tb: e.transpose(
                            pF2[:, oc * 128:(oc + 1) * 128],
                            b_dn[:, oc, tb * 128:(tb + 1) * 128], b_idb[:]),
                            waits=[dn_toks[oc], t_idb, bank_wait(pF2)],
                            inc=sp if oc == KC - 1 else None)
                    tr = E("vector", lambda e, tb=tb: e.tensor_tensor(
                        b_x1[:, tb, :], b_x1[:, tb, :], pF2[:], A.add),
                        waits=[tp], inc=sv)
                    free[id(pF2)] = tr
                    res2_toks.append(tr)
                    ln_stats(b_x1[:, tb, :], tb, [])
                St["gsl_war"][0] = tp
                out_norm = ln_batch([b_x1[:, tb, :] for tb in range(NTB)], None)
                out_toks = []
                for tb in range(NTB):
                    td_ = E("scalar", lambda e, seq=seq, t0=t0, tb=tb: e.dma_start(
                        out=d_out[seq, t0 + tb * 128:t0 + (tb + 1) * 128, :],
                        in_=b_x1[:, tb, :]), inc=dsc, inc_amt=16)
                    out_toks.append(td_)
                St["prev_out"] = out_toks

            emit_AB(0)
            for g in range(NGRP):
                if g + 1 < NGRP:
                    emit_AB(g + 1)
                emit_C(g)
                emit_D(g)
                emit_E(g)
                emit_F(g)
                emit_G(g)

            final_g = (dg, dg.cnt)
            final_s = (dsc, dsc.cnt)
            E("sync", lambda e: e.wait_ge(dg.h, final_g[1]))
            E("sync", lambda e: e.wait_ge(dsc.h, final_s[1]))

            @blk.sync
            def _(e):
                sch.replay("sync", e)

            @blk.gpsimd
            def _(e):
                sch.replay("gpsimd", e)

            @blk.vector
            def _(e):
                sch.replay("vector", e)

            @blk.scalar
            def _(e):
                sch.replay("scalar", e)

            @blk.tensor
            def _(e):
                sch.replay("tensor", e)

    return nc


_NC_CACHE = {}


def _get_nc(debug=False):
    if debug not in _NC_CACHE:
        _NC_CACHE[debug] = build_nc(debug)
    return _NC_CACHE[debug]


def kernel(L0_attn_x, L0_attn_time_decay, L0_attn_time_mix_k, L0_attn_time_mix_v,
           L0_attn_time_mix_r, L0_attn_Wk, L0_attn_Wv, L0_attn_Wr, L0_attn_Wo,
           L0_ffn_w_gate, L0_ffn_w_up, L0_ffn_w_down, _debug=False, _return_all=False):
    x = np.ascontiguousarray(np.asarray(L0_attn_x, dtype=np.float32))
    xT = np.ascontiguousarray(x.transpose(0, 2, 1).reshape(BPC * NCORES, KC, 128, S).transpose(0, 2, 1, 3))
    wvT = np.ascontiguousarray(np.asarray(L0_attn_Wv, dtype=np.float32).T)
    wrT = np.ascontiguousarray(np.asarray(L0_attn_Wr, dtype=np.float32).T)
    woT = np.ascontiguousarray(np.asarray(L0_attn_Wo, dtype=np.float32).T)
    wg = np.ascontiguousarray(np.asarray(L0_ffn_w_gate, dtype=np.float32))
    wu = np.ascontiguousarray(np.asarray(L0_ffn_w_up, dtype=np.float32))
    wd = np.ascontiguousarray(np.asarray(L0_ffn_w_down, dtype=np.float32))
    td = np.ascontiguousarray(np.asarray(L0_attn_time_decay, dtype=np.float32))
    tmv = np.ascontiguousarray(np.asarray(L0_attn_time_mix_v, dtype=np.float32))
    tmr = np.ascontiguousarray(np.asarray(L0_attn_time_mix_r, dtype=np.float32))
    ident = np.eye(128, dtype=np.float32)

    nc = _get_nc(_debug)
    in_maps = []
    for c in range(NCORES):
        in_maps.append({
            "x": np.ascontiguousarray(x[BPC * c:BPC * (c + 1)]),
            "xT": np.ascontiguousarray(xT[BPC * c:BPC * (c + 1)]),
            "wvT": wvT, "wrT": wrT, "woT": woT,
            "wg": wg, "wu": wu, "wd": wd,
            "td": td, "tmv": tmv, "tmr": tmr, "ident": ident,
        })
    res = run_bass_kernel_spmd(nc, in_maps, core_ids=list(range(NCORES)))
    out = np.concatenate([r["out"] for r in res.results], axis=0)
    if _return_all:
        return out, res.results
    return out


# version 15
# speedup vs baseline: 1.4008x; 1.4008x over previous
import contextlib
import numpy as np
import concourse.bass as bass
import concourse.mybir as mybir
from concourse.bass_utils import run_bass_kernel_spmd

F32 = mybir.dt.float32
BF16 = mybir.dt.bfloat16
A = mybir.AluOpType
AF = mybir.ActivationFunctionType
AX = mybir.AxisListType.X

B, S, H, I = 16, 2048, 1024, 2816
NCORES = 8
BPC = B // NCORES
TG = 512
GPS = S // TG
NGRP = BPC * GPS
NTB = TG // 128
KC = H // 128
ICN = I // 128
EPS = 1e-5

MIDS = (0.25, 0.75, 1.25, 1.75, 2.5, 3.5, 5.0)
WTS = (0.5, 0.5, 0.5, 0.5, 1.0, 1.0, 2.0)
NSL = [(0, 512), (512, 512), (1024, 512), (1536, 512), (2048, 512), (2560, 256)]

NEWTON_ITERS = 8


class Sem:
    __slots__ = ("name", "cnt", "h")

    def __init__(self, name):
        self.name = name
        self.cnt = 0
        self.h = None


class Sched:
    ENGS = ("sync", "gpsimd", "vector", "scalar", "tensor")

    def __init__(self):
        self.ev = {e: [] for e in self.ENGS}
        self.sems = []
        self.last_wait = {e: {} for e in self.ENGS}

    def sem(self, name):
        s = Sem(name)
        self.sems.append(s)
        return s

    def emit(self, eng, fn, waits=(), inc=None, inc_amt=1):
        real = []
        for w in waits:
            if w is None:
                continue
            s, v = w
            if v <= 0:
                continue
            if self.last_wait[eng].get(s, 0) < v:
                self.last_wait[eng][s] = v
                real.append((s, v))
        tok = None
        if inc is not None:
            inc.cnt += inc_amt
            tok = (inc, inc.cnt)
        self.ev[eng].append((fn, real, inc, inc_amt))
        return tok

    def replay(self, eng, e):
        for (fn, waits, inc, amt) in self.ev[eng]:
            for (s, v) in waits:
                e.wait_ge(s.h, v)
            inst = fn(e)
            if inc is not None:
                assert inst is not None, f"{eng}: inc without instruction"
                inst.then_inc(inc.h, amt)


def build_nc(debug=False):
    nc = bass.Bass("TRN2", target_bir_lowering=False)

    d_x = nc.dram_tensor("x", [BPC, S, H], F32, kind="ExternalInput").ap()
    d_xT = nc.dram_tensor("xT", [BPC, 128, KC, S], F32, kind="ExternalInput").ap()
    d_wvT = nc.dram_tensor("wvT", [H, H], F32, kind="ExternalInput").ap()
    d_wrT = nc.dram_tensor("wrT", [H, H], F32, kind="ExternalInput").ap()
    d_woT = nc.dram_tensor("woT", [H, H], F32, kind="ExternalInput").ap()
    d_wg = nc.dram_tensor("wg", [I, H], F32, kind="ExternalInput").ap()
    d_wu = nc.dram_tensor("wu", [I, H], F32, kind="ExternalInput").ap()
    d_wd = nc.dram_tensor("wd", [H, I], F32, kind="ExternalInput").ap()
    d_td = nc.dram_tensor("td", [H], F32, kind="ExternalInput").ap()
    d_tmv = nc.dram_tensor("tmv", [H], F32, kind="ExternalInput").ap()
    d_tmr = nc.dram_tensor("tmr", [H], F32, kind="ExternalInput").ap()
    d_ident = nc.dram_tensor("ident", [128, 128], F32, kind="ExternalInput").ap()
    d_out = nc.dram_tensor("out", [BPC, S, H], F32, kind="ExternalOutput").ap()

    sc_att = nc.dram_tensor("sc_att", [3, 128, KC, H], BF16, kind="Internal").ap()
    sc_wg = nc.dram_tensor("sc_wg", [128, KC, I], BF16, kind="Internal").ap()
    sc_wu = nc.dram_tensor("sc_wu", [128, KC, I], BF16, kind="Internal").ap()
    sc_wd = nc.dram_tensor("sc_wd", [128, ICN, H], BF16, kind="Internal").ap()

    dbg = {}
    if debug:
        for nm, shp in [
            ("dbg_y", [128, KC, TG]), ("dbg_vin", [128, KC, TG]),
            ("dbg_v", [128, KC, TG]), ("dbg_attn", [128, KC, TG]),
            ("dbg_x1", [128, NTB, H]), ("dbg_xq", [128, KC, TG]),
            ("dbg_h", [128, NTB, I]), ("dbg_dn", [128, KC, TG]),
            ("dbg_res", [128, NTB, H]), ("dbg_st", [128, NTB * 12]),
        ]:
            dbg[nm] = nc.dram_tensor(nm, shp, F32, kind="ExternalOutput").ap()
        dbg["dbg_wgq"] = nc.dram_tensor("dbg_wgq", [128, KC, I], BF16, kind="ExternalOutput").ap()
        dbg["dbg_gate"] = nc.dram_tensor("dbg_gate", [128, 2, I], F32, kind="ExternalOutput").ap()
        dbg["dbg_gsl"] = nc.dram_tensor("dbg_gsl", [128, KC, 512], BF16, kind="ExternalOutput").ap()
        dbg["dbg_hqf"] = nc.dram_tensor("dbg_hqf", [128, ICN, TG], BF16, kind="ExternalOutput").ap()
        dbg["dbg_xqf2"] = nc.dram_tensor("dbg_xqf2", [128, KC, TG], BF16, kind="ExternalOutput").ap()

    sch = Sched()
    sv = sch.sem("sv")
    ss = sch.sem("ss")
    sp = sch.sem("sp")
    dg = sch.sem("dg")
    ds = sch.sem("ds")
    dsc = sch.sem("dsc")
    sg = sch.sem("sg")
    E = sch.emit

    ctx = contextlib.ExitStack()

    def sb(name, shape, dt):
        return ctx.enter_context(nc.sbuf_tensor(name, shape, dt))

    def pst(name, shape, dt):
        return ctx.enter_context(nc.psum_tensor(name, shape, dt))

    with ctx:
        b_d = sb("b_d", [128, KC], F32)
        b_cy = sb("b_cy", [128, KC], F32)
        b_tmv = sb("b_tmv", [128, KC], F32)
        b_tmr = sb("b_tmr", [128, KC], F32)
        b_idb = sb("b_idb", [128, 128], BF16)
        b_wp = [sb(f"b_wp{i}", [128, KC, 512], BF16) for i in range(2)]
        b_xf = sb("b_xf", [128, KC, TG], BF16)
        b_y = sb("b_y", [128, KC, TG + 2], BF16)
        b_vin2 = [sb(f"b_vin{i}", [128, KC, TG], BF16) for i in range(2)]
        b_rin2 = [sb(f"b_rin{i}", [128, KC, TG], BF16) for i in range(2)]
        b_vsb = sb("b_vsb", [128, KC, TG], BF16)
        b_rs = sb("b_rs", [128, KC, TG], BF16)
        b_xtm = sb("b_xtm", [128, 2, H], F32)
        b_x1 = sb("b_x1", [128, NTB, H], F32)
        b_abs = sb("b_abs", [128, H], F32)
        b_w = sb("b_w", [128, H], F32)
        b_sgn = sb("b_sgn", [128, H], BF16)
        b_mA = sb("b_mA", [128, H], BF16)
        b_mB = sb("b_mB", [128, H], BF16)
        b_mm = sb("b_mm", [128, H], BF16)
        b_bm = sb("b_bm", [128, 176], F32)
        b_sc = sb("b_sc", [128, 176], F32)
        b_rc = sb("b_rc", [128, 176], F32)
        b_sb16 = sb("b_sb16", [128, 176], BF16)
        b_st = sb("b_st", [128, NTB * 12], F32)
        b_xq = sb("b_xq", [128, H], BF16)
        b_xqf = sb("b_xqf", [128, KC, TG], BF16)
        b_gsl = [sb(f"b_gsl{i}", [128, KC, 512], BF16) for i in range(2)]
        b_usl = [sb(f"b_usl{i}", [128, KC, 512], BF16) for i in range(2)]
        b_dsl = [sb(f"b_dsl{i}", [128, ICN, 128], BF16) for i in range(2)]
        b_h = sb("b_h", [128, 512], F32)
        b_hq = sb("b_hq", [128, 512], BF16)
        b_sig2 = [sb(f"b_sig2_{i}", [128, 512], F32) for i in range(2)]
        b_hqf = sb("b_hqf", [128, ICN, TG], BF16)

        _rsf = b_rs[:].rearrange("p kc t -> p (kc t)")
        b_s4a = _rsf[:, 0:H]
        b_s4b = _rsf[:, H:2 * H]
        b_mC2 = _rsf[:, 2 * H:3 * H]
        b_dn = b_gsl[0][:].rearrange("p kc n -> p kc n") if False else b_gsl[0]
        b_wnat = b_x1[:].rearrange("p tb h -> p (tb h)")[:, :I]
        b_sq = b_w[:]
        b_stage = b_xqf[:, :, 0:128]
        b_t = b_abs[:, 512:1024]

        pA = pst("pA", [128, 512], F32)
        pB = pst("pB", [128, 512], F32)
        pC = pst("pC", [128, 512], F32)
        pD = pst("pD", [128, 512], F32)
        pG = pst("pG", [128, 512], F32)
        pE = pst("pE", [128, 1024], BF16)
        pF = pst("pF", [128, 1024], BF16)
        pF2 = pst("pF2", [128, 1024], BF16)

        sem_ctx = contextlib.ExitStack()
        with sem_ctx:
            for s_ in sch.sems:
                s_.h = sem_ctx.enter_context(nc.semaphore(s_.name))
            blk = ctx.enter_context(nc.Block())

            free = {}
            qd_war = [None]

            def bank_wait(p):
                return free.get(id(p))

            def qd_tile(src_ap, dst_ap, fd, wait_src, dst_war=None):
                W = fd // 16
                sv3 = src_ap.rearrange("p (w k) -> p w k", k=16)
                war = qd_war[0]
                tv = E("vector", lambda e: e.tensor_reduce(
                    b_bm[:, :W], sv3, AX, A.max, apply_absolute_value=True),
                    waits=[wait_src, war])
                E("vector", lambda e: e.tensor_copy(b_st[:, 40:44], b_st[:, 40:44]))
                ts = E("scalar", lambda e: e.activation(b_abs[:, :fd], src_ap, AF.Abs),
                       waits=[wait_src, war], inc=ss)
                ts2 = E("scalar", lambda e: e.activation(b_sgn[:, :fd], src_ap, AF.Sign),
                        inc=ss)
                E("vector", lambda e: e.tensor_copy(b_st[:, 8:40], b_st[:, 8:40]))
                E("vector", lambda e: e.tensor_scalar(
                    b_sc[:, :W], b_bm[:, :W], 1.0 / 6.0, 1e-12, A.mult, A.max))
                E("vector", lambda e: e.tensor_copy(b_st[:, 8:40], b_st[:, 8:40]))
                E("vector", lambda e: e.reciprocal(b_rc[:, :W], b_sc[:, :W]))
                E("vector", lambda e: e.tensor_copy(b_sb16[:, :W], b_sc[:, :W]))
                r_b = b_rc[:, :W].rearrange("p (w o) -> p w o", o=1).broadcast_to([128, W, 16])
                s_b = b_sb16[:, :W].rearrange("p (w o) -> p w o", o=1).broadcast_to([128, W, 16])
                wv3 = b_w[:, :fd].rearrange("p (w k) -> p w k", k=16)
                av3 = b_abs[:, :fd].rearrange("p (w k) -> p w k", k=16)
                E("vector", lambda e: e.tensor_tensor(wv3, av3, r_b, A.mult), waits=[ts])
                wap = b_w[:, :fd]
                E("vector", lambda e: e.tensor_scalar(b_mA[:, :fd], wap, MIDS[0], WTS[0], A.is_gt, A.mult))
                for k in range(1, 7):
                    E("vector", lambda e, k=k: e.tensor_scalar(b_mB[:, :fd], wap, MIDS[k], WTS[k], A.is_gt, A.mult))
                    E("vector", lambda e: e.tensor_tensor(b_mA[:, :fd], b_mA[:, :fd], b_mB[:, :fd], A.add))
                mm3 = b_mm[:, :fd].rearrange("p (w k) -> p w k", k=16)
                g3 = b_sgn[:, :fd].rearrange("p (w k) -> p w k", k=16)
                E("vector", lambda e: e.tensor_tensor(mm3, g3, s_b, A.mult), waits=[ts2])
                tq = E("vector", lambda e: e.tensor_tensor(
                    dst_ap, b_mA[:, :fd], b_mm[:, :fd], A.mult),
                    waits=[dst_war], inc=sv)
                qd_war[0] = tq
                return tq

            def ln_stats(src_full, tb, waits_v):
                c = tb * 12
                E("vector", lambda e: e.tensor_reduce(
                    b_st[:, c:c + 1], src_full, AX, A.add), waits=waits_v)
                E("vector", lambda e: e.scalar_tensor_tensor(
                    b_sq, src_full, 1.0, src_full, A.mult, A.mult,
                    accum_out=b_st[:, c + 1:c + 2]))

            def ln_batch(dst_srcs, waits_dst, prior=None):
                nt = len(dst_srcs)
                E("vector", lambda e: e.tensor_copy(b_st[:, 44:48], b_st[:, 44:48]))

                def step(fn):
                    for tb in range(nt):
                        fn(tb, tb * 12)
                for tb in range(nt):
                    if prior:
                        E("vector", lambda e, tb=tb: e.tensor_single_scalar(
                            b_st[:, tb * 12 + 2:tb * 12 + 3], b_st[:, tb * 12:tb * 12 + 1],
                            1.0 / H, A.mult), waits=[prior[tb]])
                    else:
                        E("vector", lambda e, tb=tb: e.tensor_single_scalar(
                            b_st[:, tb * 12 + 2:tb * 12 + 3], b_st[:, tb * 12:tb * 12 + 1],
                            1.0 / H, A.mult))
                step(lambda tb, c: E("vector", lambda e: e.tensor_single_scalar(
                    b_st[:, c + 3:c + 4], b_st[:, c + 1:c + 2], 1.0 / H, A.mult)))
                step(lambda tb, c: E("vector", lambda e: e.tensor_tensor(
                    b_st[:, c + 4:c + 5], b_st[:, c + 2:c + 3], b_st[:, c + 2:c + 3], A.mult)))
                step(lambda tb, c: E("vector", lambda e: e.tensor_tensor(
                    b_st[:, c + 3:c + 4], b_st[:, c + 3:c + 4], b_st[:, c + 4:c + 5], A.subtract)))
                step(lambda tb, c: E("vector", lambda e: e.tensor_single_scalar(
                    b_st[:, c + 3:c + 4], b_st[:, c + 3:c + 4], EPS, A.add)))
                step(lambda tb, c: E("vector", lambda e: e.tensor_scalar(
                    b_st[:, c + 6:c + 7], b_st[:, c + 3:c + 4], 0.5, 0.5, A.mult, A.add)))
                step(lambda tb, c: E("vector", lambda e: e.reciprocal(
                    b_st[:, c + 5:c + 6], b_st[:, c + 6:c + 7])))
                for _ in range(NEWTON_ITERS):
                    step(lambda tb, c: E("vector", lambda e: e.tensor_tensor(
                        b_st[:, c + 6:c + 7], b_st[:, c + 5:c + 6], b_st[:, c + 5:c + 6], A.mult)))
                    step(lambda tb, c: E("vector", lambda e: e.tensor_tensor(
                        b_st[:, c + 6:c + 7], b_st[:, c + 6:c + 7], b_st[:, c + 3:c + 4], A.mult)))
                    step(lambda tb, c: E("vector", lambda e: e.tensor_scalar(
                        b_st[:, c + 6:c + 7], b_st[:, c + 6:c + 7], -0.5, 1.5, A.mult, A.add)))
                    step(lambda tb, c: E("vector", lambda e: e.tensor_tensor(
                        b_st[:, c + 5:c + 6], b_st[:, c + 5:c + 6], b_st[:, c + 6:c + 7], A.mult)))
                step(lambda tb, c: E("vector", lambda e: e.tensor_tensor(
                    b_st[:, c + 7:c + 8], b_st[:, c + 2:c + 3], b_st[:, c + 5:c + 6], A.mult)))
                tns = []
                for tb in range(nt):
                    c = tb * 12
                    tns.append(E("vector", lambda e, c=c: e.tensor_single_scalar(
                        b_st[:, c + 7:c + 8], b_st[:, c + 7:c + 8], -1.0, A.mult), inc=sv))
                outs = []
                for tb in range(nt):
                    c = tb * 12
                    w_extra = waits_dst[tb] if waits_dst else None
                    outs.append(E("scalar", lambda e, tb=tb, c=c: e.activation(
                        dst_srcs[tb], dst_srcs[tb], AF.Identity,
                        bias=b_st[:, c + 7:c + 8], scale=b_st[:, c + 5:c + 6]),
                        waits=[tns[tb], w_extra], inc=ss))
                return outs

            t_d = E("sync", lambda e: e.dma_start(
                out=b_d[:], in_=d_td.rearrange("(kc p) -> p kc", p=128),
                allow_slow_non_contiguous=True), inc=ds, inc_amt=16)
            t_tmv = E("sync", lambda e: e.dma_start(
                out=b_tmv[:], in_=d_tmv.rearrange("(kc p) -> p kc", p=128),
                allow_slow_non_contiguous=True), inc=ds, inc_amt=16)
            t_tmr = E("sync", lambda e: e.dma_start(
                out=b_tmr[:], in_=d_tmr.rearrange("(kc p) -> p kc", p=128),
                allow_slow_non_contiguous=True), inc=ds, inc_amt=16)
            t_id = E("sync", lambda e: e.dma_start(out=b_bm[:, :128].rearrange("p (a b) -> p a b", b=128) if False else b_bm[:, :128], in_=d_ident[:]),
                     inc=ds, inc_amt=16)
            E("scalar", lambda e: e.activation(b_d[:], b_d[:], AF.Exp), waits=[t_d])
            E("scalar", lambda e: e.activation(b_tmv[:], b_tmv[:], AF.Identity),
              waits=[t_tmv])
            t_dd = E("scalar", lambda e: e.activation(b_d[:], b_d[:], AF.Exp, scale=-1.0),
                     inc=ss)
            t_idb = E("vector", lambda e: e.tensor_copy(b_idb[:], b_bm[:, :128]),
                      waits=[t_id], inc=sv)
            for k in range(4):
                E("vector", lambda e, k=k: e.memset(b_st[:, 44 + k:45 + k], -MIDS[k]))

            t_att = []
            bounce_war = [None, None]
            for pi, dw in enumerate((d_wvT, d_wrT, d_woT)):
                last = None
                for kc in range(KC):
                    bi = (pi * KC + kc) % 2
                    tl = E("gpsimd", lambda e, dw=dw, kc=kc, bi=bi: e.dma_start(
                        out=b_wp[bi][:].rearrange("p kc m -> p (kc m)")[:, :H],
                        in_=dw[kc * 128:(kc + 1) * 128, :]),
                        waits=[bounce_war[bi]], inc=dg, inc_amt=16)
                    last = E("sync", lambda e, pi=pi, kc=kc, bi=bi: e.dma_start(
                        out=sc_att[pi, :, kc, :],
                        in_=b_wp[bi][:].rearrange("p kc m -> p (kc m)")[:, :H]),
                        waits=[tl], inc=ds, inc_amt=16)
                    bounce_war[bi] = last
                t_att.append(last)
            wp_war = [bounce_war[0], bounce_war[1]]

            pf_war = [None]
            stage_war = None
            wnat_war = None
            for dsrc, dscr in ((d_wg, sc_wg), (d_wu, sc_wu)):
                for ch in range(ICN):
                    tl = E("sync", lambda e, dsrc=dsrc, ch=ch: e.dma_start(
                        out=b_wnat[:, :H], in_=dsrc[ch * 128:(ch + 1) * 128, :]),
                        waits=[wnat_war], inc=ds, inc_amt=16)
                    tq = qd_tile(b_wnat[:, :H], b_xq[:], H, tl, dst_war=pf_war[0])
                    wnat_war = tq
                    tp = None
                    for hc in range(KC):
                        tp = E("tensor", lambda e, hc=hc: e.transpose(
                            pF[:, hc * 128:(hc + 1) * 128],
                            b_xq[:, hc * 128:(hc + 1) * 128], b_idb[:]),
                            waits=[tq, t_idb, bank_wait(pF)],
                            inc=sp if hc == KC - 1 else None)
                    te = E("scalar", lambda e: e.copy(
                        b_stage, pF[:].rearrange("p (kc n) -> p kc n", n=128)),
                        waits=[tp, stage_war], inc=ss)
                    free[id(pF)] = te
                    pf_war[0] = tp
                    stage_war = E("sync", lambda e, dscr=dscr, ch=ch: e.dma_start(
                        out=dscr[:, :, ch * 128:(ch + 1) * 128], in_=b_stage),
                        waits=[te], inc=ds, inc_amt=16)
                    if debug and dscr is sc_wg:
                        stage_war = E("gpsimd", lambda e, ch=ch: e.dma_start(
                            out=dbg["dbg_wgq"][:, :, ch * 128:(ch + 1) * 128],
                            in_=b_stage), waits=[te], inc=dg, inc_amt=16)
            xq_war_seed = pf_war[0]
            dn_prep_war = None
            for ch in range(KC):
                tl = E("sync", lambda e, ch=ch: e.dma_start(
                    out=b_wnat[:, :I], in_=d_wd[ch * 128:(ch + 1) * 128, :]),
                    waits=[wnat_war], inc=ds, inc_amt=16)
                dq = b_dn[:].rearrange("p kc t -> p (kc t)")
                piece = []
                for oi, off in enumerate((0, 1024, 2048)):
                    fd = min(1024, I - off)
                    piece.append(qd_tile(b_wnat[:, off:off + fd], dq[:, off:off + fd],
                                         fd, tl, dst_war=dn_prep_war))
                wnat_war = piece[-1]
                for r0 in range(0, ICN, 8):
                    nt = min(8, ICN - r0)
                    tp = None
                    for j in range(nt):
                        ic = r0 + j
                        tp = E("tensor", lambda e, ic=ic, j=j: e.transpose(
                            pF[:, j * 128:(j + 1) * 128],
                            dq[:, ic * 128:(ic + 1) * 128], b_idb[:]),
                            waits=piece + [t_idb, bank_wait(pF)],
                            inc=sp if j == nt - 1 else None)
                    dn_prep_war = tp
                    te = E("scalar", lambda e, nt=nt: e.copy(
                        b_stage[:, :nt, :], pF[:, :nt * 128].rearrange("p (kc n) -> p kc n", n=128)),
                        waits=[tp, stage_war], inc=ss)
                    free[id(pF)] = te
                    stage_war = E("sync", lambda e, ch=ch, r0=r0, nt=nt: e.dma_start(
                        out=sc_wd[:, r0:r0 + nt, ch * 128:(ch + 1) * 128],
                        in_=b_stage[:, :nt, :]),
                        waits=[te], inc=ds, inc_amt=16)
            t_prep_done = (ds, ds.cnt)
            wnat_guard = wnat_war
            stage_guard = stage_war

            E("vector", lambda e: e.memzero(b_y[:, :, 0:2]))
            t_carry = E("vector", lambda e: e.memzero(b_cy[:]), inc=sv)

            St = {
                "prev_mix": None, "prev_out": [], "war_ab": [None, None],
                "xq_war": xq_war_seed, "gsl_war": [None, None],
                "usl_war": [None, None], "dsl_war": [None, None],
                "t_carry": t_carry, "t_xf": None, "xtm_war": [None, None],
                "mix": [None] * NGRP, "attn_toks": {}, "evac_v": {}, "evac_r": {},
                "x1_toks": {}, "xqf_toks": {}, "hq_toks": {}, "dn_toks": {},
                "first_out": True, "last_tt": [None, None],
            }

            def emit_AB(g):
                seq = g // GPS
                t0 = (g % GPS) * TG
                p = g % 2
                b_vin, b_rin = b_vin2[p], b_rin2[p]
                t_xf = E("gpsimd", lambda e, seq=seq, t0=t0: e.dma_start(
                    out=b_xf[:], in_=d_xT[seq, :, :, t0:t0 + TG]),
                    waits=[St["prev_mix"]], inc=dg, inc_amt=16)
                if g > 0:
                    if g % GPS == 0:
                        E("vector", lambda e: e.memzero(b_y[:, :, 0:2]))
                        St["t_carry"] = E("vector", lambda e: e.memzero(b_cy[:]), inc=sv)
                    else:
                        E("vector", lambda e: e.tensor_copy(
                            b_y[:, :, 0:1], b_y[:, :, TG:TG + 1]))
                        E("vector", lambda e: e.tensor_copy(
                            b_cy[:].rearrange("p (kc o) -> p kc o", o=1),
                            b_y[:, :, TG:TG + 1]))
                        St["t_carry"] = E("vector", lambda e: e.tensor_copy(
                            b_st[:, 8:40], b_st[:, 8:40]), inc=sv)
                for kc in range(KC):
                    d_bb = b_d[:, kc:kc + 1].broadcast_to([128, TG])
                    E("vector", lambda e, kc=kc, d_bb=d_bb: e.tensor_tensor_scan(
                        b_y[:, kc, 1:TG + 1], d_bb, b_xf[:, kc, :],
                        b_cy[:, kc:kc + 1], A.mult, A.add),
                        waits=[t_xf, St["t_carry"], t_dd])
                war = St["war_ab"][p]
                for kc in range(KC):
                    E("vector", lambda e, kc=kc, b_rin=b_rin: e.tensor_tensor(
                        b_rin[:, kc, :], b_xf[:, kc, :], b_y[:, kc, 0:TG], A.subtract),
                        waits=[war])
                    E("vector", lambda e, kc=kc, b_vin=b_vin, b_rin=b_rin: e.scalar_tensor_tensor(
                        b_vin[:, kc, :], b_rin[:, kc, :], b_tmv[:, kc:kc + 1],
                        b_y[:, kc, 0:TG], A.mult, A.add), waits=[t_tmv])
                    E("vector", lambda e, kc=kc, b_rin=b_rin: e.scalar_tensor_tensor(
                        b_rin[:, kc, :], b_rin[:, kc, :], b_tmr[:, kc:kc + 1],
                        b_y[:, kc, 0:TG], A.mult, A.add), waits=[t_tmr])
                mix = E("vector", lambda e: e.tensor_copy(b_cy[:, 0:1], b_cy[:, 0:1]), inc=sv)
                St["prev_mix"] = mix
                St["mix"][g] = mix
                if debug and g == 0:
                    E("gpsimd", lambda e: e.dma_start(out=dbg["dbg_y"], in_=b_y[:, :, 1:TG + 1]),
                      waits=[mix], inc=dg, inc_amt=16)
                    E("gpsimd", lambda e, b_vin=b_vin: e.dma_start(out=dbg["dbg_vin"], in_=b_vin[:]),
                      waits=[mix], inc=dg, inc_amt=16)

            def emit_C(g):
                p = g % 2
                b_vin, b_rin = b_vin2[p], b_rin2[p]
                b_attn = b_vin
                mix = St["mix"][g]
                rs_guard = qd_war[0]
                banks = [pA, pB]
                evac_v = []
                evac_r = []
                attn_toks = []
                for item in range(6):
                    proj, half = divmod(item, 2)
                    bi = item % 2
                    tload = E("sync", lambda e, proj=proj, half=half, bi=bi: e.dma_start(
                        out=b_wp[bi][:],
                        in_=sc_att[proj, :, :, half * 512:(half + 1) * 512]),
                        waits=[t_att[proj], wp_war[bi]], inc=ds, inc_amt=16)
                    last_mm = None
                    for mc4 in range(4):
                        mc = half * 4 + mc4
                        bank = banks[mc % 2]
                        tmm = None
                        for kc in range(KC):
                            if proj < 2:
                                rhs = b_vin if proj == 0 else b_rin
                                tmm = E("tensor", lambda e, bi=bi, kc=kc, mc4=mc4, bank=bank, rhs=rhs: e.matmul(
                                    bank[:], b_wp[bi][:, kc, mc4 * 128:(mc4 + 1) * 128],
                                    rhs[:, kc, :], start=(kc == 0), stop=(kc == KC - 1)),
                                    waits=[mix, tload, bank_wait(bank)],
                                    inc=sp if kc == KC - 1 else None)
                            else:
                                tmm = E("tensor", lambda e, bi=bi, kc=kc, mc4=mc4, bank=bank: e.matmul(
                                    bank[:], b_wp[bi][:, kc, mc4 * 128:(mc4 + 1) * 128],
                                    b_vsb[:, kc, :], start=(kc == 0), stop=(kc == KC - 1)),
                                    waits=[tload, bank_wait(bank)] + evac_v,
                                    inc=sp if kc == KC - 1 else None)
                        last_mm = tmm
                        if proj == 0:
                            te = E("scalar", lambda e, mc=mc, bank=bank: e.copy(
                                b_vsb[:, mc, :], bank[:]), waits=[tmm], inc=ss)
                            evac_v.append(te)
                            free[id(bank)] = te
                        elif proj == 1:
                            te = E("scalar", lambda e, mc=mc, bank=bank: e.activation(
                                b_rs[:, mc, :], bank[:], AF.Sigmoid),
                                waits=[tmm, rs_guard], inc=ss)
                            evac_r.append(te)
                            free[id(bank)] = te
                        else:
                            ta = E("vector", lambda e, mc=mc, bank=bank, b_attn=b_attn: e.tensor_tensor(
                                b_attn[:, mc, :], b_rs[:, mc, :], bank[:], A.mult),
                                waits=[tmm, evac_r[mc]], inc=sv)
                            attn_toks.append(ta)
                            free[id(bank)] = ta
                    wp_war[bi] = last_mm
                St["attn_toks"][g] = attn_toks
                St["evac_v"][g] = evac_v
                if debug and g == 0:
                    E("gpsimd", lambda e: e.dma_start(out=dbg["dbg_v"], in_=b_vsb[:]),
                      waits=evac_v, inc=dg, inc_amt=16)
                    E("gpsimd", lambda e, b_attn=b_attn: e.dma_start(out=dbg["dbg_attn"], in_=b_attn[:]),
                      waits=attn_toks, inc=dg, inc_amt=16)

            def emit_D(g):
                p = g % 2
                seq = g // GPS
                gtb0 = (g % GPS) * NTB
                b_attn = b_vin2[p]
                attn_toks = St["attn_toks"][g]
                prev_out = St["prev_out"]
                res_toks = []
                tp = None

                def xtm_load(tb):
                    return E("sync", lambda e, seq=seq, gtb0=gtb0, tb=tb: e.dma_start(
                        out=b_xtm[:, tb % 2, :],
                        in_=d_x.rearrange("s (tb p) h -> s p tb h", p=128)[seq, :, gtb0 + tb, :]),
                        waits=[St["xtm_war"][tb % 2]], inc=ds, inc_amt=16)
                xtm_toks = [xtm_load(0), xtm_load(1), None, None]
                for tb in range(NTB):
                    for oc in range(KC):
                        tp = E("tensor", lambda e, oc=oc, tb=tb, b_attn=b_attn: e.transpose(
                            pE[:, oc * 128:(oc + 1) * 128],
                            b_attn[:, oc, tb * 128:(tb + 1) * 128], b_idb[:]),
                            waits=[attn_toks[oc], t_idb, bank_wait(pE)],
                            inc=sp if oc == KC - 1 else None)
                    tr = E("vector", lambda e, tb=tb: e.tensor_tensor(
                        b_x1[:, tb, :], b_xtm[:, tb % 2, :], pE[:], A.add),
                        waits=[tp, xtm_toks[tb],
                               (prev_out[tb] if prev_out else wnat_guard)], inc=sv)
                    free[id(pE)] = tr
                    St["xtm_war"][tb % 2] = tr
                    if tb + 2 < NTB:
                        xtm_toks[tb + 2] = xtm_load(tb + 2)
                    res_toks.append(tr)
                    ln_stats(b_x1[:, tb, :], tb, [])
                    if debug and g == 0:
                        tdmp = E("gpsimd", lambda e, tb=tb: e.dma_start(
                            out=dbg["dbg_res"][:, tb, :], in_=b_x1[:, tb, :]),
                            waits=[tr], inc=dg, inc_amt=16)
                        res_toks[-1] = tdmp
                St["war_ab"][p] = tp
                x1_toks = ln_batch([b_x1[:, tb, :] for tb in range(NTB)],
                                   res_toks if debug else None,
                                   prior=St.get("prev_norm"))
                St["x1_toks"][g] = x1_toks
                if debug and g == 0:
                    E("gpsimd", lambda e: e.dma_start(out=dbg["dbg_x1"], in_=b_x1[:]),
                      waits=x1_toks, inc=dg, inc_amt=16)
                    E("gpsimd", lambda e: e.dma_start(out=dbg["dbg_st"], in_=b_st[:]),
                      waits=x1_toks, inc=dg, inc_amt=16)

            def emit_E(g):
                xqf_toks = []
                x1_toks = St["x1_toks"][g]
                for tb in range(NTB):
                    tq = qd_tile(b_x1[:, tb, :], b_xq[:], H, x1_toks[tb],
                                 dst_war=St["xq_war"])
                    tp = None
                    for kc in range(KC):
                        tp = E("tensor", lambda e, kc=kc: e.transpose(
                            pF[:, kc * 128:(kc + 1) * 128],
                            b_xq[:, kc * 128:(kc + 1) * 128], b_idb[:]),
                            waits=[tq, t_idb, bank_wait(pF)],
                            inc=sp if kc == KC - 1 else None)
                    St["xq_war"] = tp
                    te = E("scalar", lambda e, tb=tb: e.copy(
                        b_xqf[:, :, tb * 128:(tb + 1) * 128],
                        pF[:].rearrange("p (kc t) -> p kc t", t=128)),
                        waits=[tp, stage_guard if g == 0 and tb == 0 else None], inc=ss)
                    free[id(pF)] = te
                    xqf_toks.append(te)
                St["xqf_toks"][g] = xqf_toks
                if debug and g == 0:
                    E("gpsimd", lambda e: e.dma_start(out=dbg["dbg_xq"], in_=b_xqf[:]),
                      waits=xqf_toks, inc=dg, inc_amt=16)

            def emit_F(g):
                hq_toks = []
                xqf_toks = St["xqf_toks"][g]
                fbanks = [(pA, pB), (pC, pD)]
                prev_sig = None
                last_tt = St["last_tt"]
                gsl_war = St["gsl_war"]
                usl_war = St["usl_war"]
                hq_dst_war = [None]
                iters = [(si, n0, ns, tb) for si, (n0, ns) in enumerate(NSL)
                         for tb in range(NTB)]
                stash = {}

                def stage1(it):
                    si, n0, ns, tb = iters[it]
                    if tb == 0:
                        tgd = E("sync", lambda e, si=si, n0=n0, ns=ns: e.dma_start(
                            out=b_gsl[si % 2][:, :, :ns],
                            in_=sc_wg[:, :, n0:n0 + ns]),
                            waits=[t_prep_done if g == 0 else None, gsl_war[si % 2]],
                            inc=ds, inc_amt=16)
                        tud = E("sync", lambda e, si=si, n0=n0, ns=ns: e.dma_start(
                            out=b_usl[si % 2][:, :, :ns],
                            in_=sc_wu[:, :, n0:n0 + ns]),
                            waits=[usl_war[si % 2]], inc=ds, inc_amt=16)
                        stash[("dma", si)] = (tgd, tud)
                        if debug and g == 0 and si == 0:
                            E("gpsimd", lambda e: e.dma_start(out=dbg["dbg_gsl"], in_=b_gsl[0][:]),
                              waits=[tgd], inc=dg, inc_amt=16)
                            E("gpsimd", lambda e: e.dma_start(out=dbg["dbg_xqf2"], in_=b_xqf[:]),
                              waits=[xqf_toks[3]], inc=dg, inc_amt=16)
                    tgd, tud = stash[("dma", si)]
                    bg, bu = fbanks[it % 2]
                    tmmg = tmmu = None
                    for kc in range(KC):
                        tmmg = E("tensor", lambda e, kc=kc, tb=tb, si=si, ns=ns, bg=bg: e.matmul(
                            bg[:, :ns], b_xqf[:, kc, tb * 128:(tb + 1) * 128],
                            b_gsl[si % 2][:, kc, :ns],
                            start=(kc == 0), stop=(kc == KC - 1)),
                            waits=[xqf_toks[tb], tgd, bank_wait(bg)],
                            inc=sp if kc == KC - 1 else None)
                        tmmu = E("tensor", lambda e, kc=kc, tb=tb, si=si, ns=ns, bu=bu: e.matmul(
                            bu[:, :ns], b_xqf[:, kc, tb * 128:(tb + 1) * 128],
                            b_usl[si % 2][:, kc, :ns],
                            start=(kc == 0), stop=(kc == KC - 1)),
                            waits=[tud, bank_wait(bu)],
                            inc=sp if kc == KC - 1 else None)
                    if tb == NTB - 1:
                        gsl_war[si % 2] = tmmg
                        usl_war[si % 2] = tmmu
                    stash[("s1", it)] = (tmmg, tmmu, bg, bu)

                def flush_evac(it):
                    tp_, tb_, n0_, ntl_, pFx_ = stash.pop(("ev", it))
                    te = E("scalar", lambda e, tb=tb_, n0=n0_, ntl=ntl_, pFx=pFx_: e.copy(
                        b_hqf[:, n0 // 128:n0 // 128 + ntl, tb * 128:(tb + 1) * 128],
                        pFx[:, :ntl * 128].rearrange("p (ic t) -> p ic t", t=128)),
                        waits=[tp_], inc=ss)
                    free[id(pFx_)] = te
                    hq_toks.append(te)

                def stage2(it):
                    si, n0, ns, tb = iters[it]
                    tmmg, tmmu, bg, bu = stash.pop(("s1", it))
                    tsg = E("scalar", lambda e, ns=ns, bg=bg, it=it: e.activation(
                        b_sig2[it % 2][:, :ns], bg[:, :ns], AF.Sigmoid),
                        waits=[tmmg, last_tt[it % 2]], inc=ss)
                    tt = E("vector", lambda e, ns=ns, bu=bu, it=it: e.tensor_tensor(
                        b_t[:, :ns], b_sig2[it % 2][:, :ns], bu[:, :ns], A.mult),
                        waits=[tsg, tmmu], inc=sv)
                    free[id(bu)] = tt
                    last_tt[it % 2] = tt
                    th = E("vector", lambda e, ns=ns, bg=bg: e.tensor_tensor(
                        b_h[:, :ns], b_t[:, :ns], bg[:, :ns], A.mult),
                        waits=[tt], inc=sv)
                    free[id(bg)] = th
                    tqh = qd_tile(b_h[:, :ns], b_hq[:, :ns], ns, th,
                                  dst_war=hq_dst_war[0])
                    pFx = [pF, pF2][it % 2]
                    ntl = ns // 128
                    tp = None
                    for j in range(ntl):
                        tp = E("tensor", lambda e, j=j, pFx=pFx: e.transpose(
                            pFx[:, j * 128:(j + 1) * 128],
                            b_hq[:, j * 128:(j + 1) * 128], b_idb[:]),
                            waits=[tqh, t_idb, bank_wait(pFx)],
                            inc=sp if j == ntl - 1 else None)
                    hq_dst_war[0] = tp
                    stash[("ev", it)] = (tp, tb, n0, ntl, pFx)
                    if ("ev", it - 1) in stash:
                        flush_evac(it - 1)
                    if debug and g == 0:
                        E("gpsimd", lambda e, tb=tb, n0=n0, ns=ns: e.dma_start(
                            out=dbg["dbg_h"][:, tb, n0:n0 + ns], in_=b_h[:, :ns]),
                            waits=[tqh], inc=dg, inc_amt=16)

                stage1(0)
                for it in range(1, len(iters)):
                    stage1(it)
                    stage2(it - 1)
                stage2(len(iters) - 1)
                flush_evac(len(iters) - 1)
                St["hq_toks"][g] = hq_toks

            def emit_G(g):
                seq = g // GPS
                t0 = (g % GPS) * TG
                hq_toks = St["hq_toks"][g]
                dsl_war = St["dsl_war"]
                hq_dump_tok = None
                if debug and g == 0:
                    hq_dump_tok = E("gpsimd", lambda e: e.dma_start(
                        out=dbg["dbg_hqf"], in_=b_hqf[:]), waits=hq_toks, inc=dg, inc_amt=16)
                dn_toks = []
                dbanks = [pG, pA]
                for oc in range(KC):
                    tdw = E("sync", lambda e, oc=oc: e.dma_start(
                        out=b_dsl[oc % 2][:],
                        in_=sc_wd[:, :, oc * 128:(oc + 1) * 128]),
                        waits=[t_prep_done if g == 0 else None, dsl_war[oc % 2]],
                        inc=ds, inc_amt=16)
                    bank = dbanks[oc % 2]
                    tmm = None
                    for ic in range(ICN):
                        tmm = E("tensor", lambda e, ic=ic, oc=oc, bank=bank: e.matmul(
                            bank[:], b_dsl[oc % 2][:, ic, :], b_hqf[:, ic, :],
                            start=(ic == 0), stop=(ic == ICN - 1)),
                            waits=hq_toks + [tdw, bank_wait(bank), hq_dump_tok],
                            inc=sp if ic == ICN - 1 else None)
                    dsl_war[oc % 2] = tmm
                    te = E("vector", lambda e, oc=oc, bank=bank: e.tensor_copy(
                        b_dn[:, oc, :], bank[:]), waits=[tmm], inc=sv)
                    free[id(bank)] = te
                    dn_toks.append(te)
                if debug and g == 0:
                    E("gpsimd", lambda e: e.dma_start(out=dbg["dbg_dn"], in_=b_dn[:]),
                      waits=dn_toks, inc=dg, inc_amt=16)
                out_toks = []
                res2_toks = []
                for tb in range(NTB):
                    tp = None
                    for oc in range(KC):
                        tp = E("tensor", lambda e, oc=oc, tb=tb: e.transpose(
                            pF2[:, oc * 128:(oc + 1) * 128],
                            b_dn[:, oc, tb * 128:(tb + 1) * 128], b_idb[:]),
                            waits=[dn_toks[oc], t_idb, bank_wait(pF2)],
                            inc=sp if oc == KC - 1 else None)
                    tr = E("vector", lambda e, tb=tb: e.tensor_tensor(
                        b_x1[:, tb, :], b_x1[:, tb, :], pF2[:], A.add),
                        waits=[tp], inc=sv)
                    free[id(pF2)] = tr
                    res2_toks.append(tr)
                    ln_stats(b_x1[:, tb, :], tb, [])
                St["gsl_war"][0] = tp
                out_norm = ln_batch([b_x1[:, tb, :] for tb in range(NTB)], None,
                                    prior=St["x1_toks"][g])
                St["prev_norm"] = out_norm
                out_toks = []
                for tb in range(NTB):
                    td_ = E("scalar", lambda e, seq=seq, t0=t0, tb=tb: e.dma_start(
                        out=d_out[seq, t0 + tb * 128:t0 + (tb + 1) * 128, :],
                        in_=b_x1[:, tb, :]), inc=dsc, inc_amt=16)
                    out_toks.append(td_)
                St["prev_out"] = out_toks

            emit_AB(0)
            for g in range(NGRP):
                if g + 1 < NGRP:
                    emit_AB(g + 1)
                emit_C(g)
                if g > 0:
                    emit_G(g - 1)
                emit_D(g)
                emit_E(g)
                emit_F(g)
            emit_G(NGRP - 1)

            final_g = (dg, dg.cnt)
            final_s = (dsc, dsc.cnt)
            E("sync", lambda e: e.wait_ge(dg.h, final_g[1]))
            E("sync", lambda e: e.wait_ge(dsc.h, final_s[1]))

            @blk.sync
            def _(e):
                sch.replay("sync", e)

            @blk.gpsimd
            def _(e):
                sch.replay("gpsimd", e)

            @blk.vector
            def _(e):
                sch.replay("vector", e)

            @blk.scalar
            def _(e):
                sch.replay("scalar", e)

            @blk.tensor
            def _(e):
                sch.replay("tensor", e)

    return nc


_NC_CACHE = {}


def _get_nc(debug=False):
    if debug not in _NC_CACHE:
        _NC_CACHE[debug] = build_nc(debug)
    return _NC_CACHE[debug]


def kernel(L0_attn_x, L0_attn_time_decay, L0_attn_time_mix_k, L0_attn_time_mix_v,
           L0_attn_time_mix_r, L0_attn_Wk, L0_attn_Wv, L0_attn_Wr, L0_attn_Wo,
           L0_ffn_w_gate, L0_ffn_w_up, L0_ffn_w_down, _debug=False, _return_all=False):
    x = np.ascontiguousarray(np.asarray(L0_attn_x, dtype=np.float32))
    xT = np.ascontiguousarray(x.transpose(0, 2, 1).reshape(BPC * NCORES, KC, 128, S).transpose(0, 2, 1, 3))
    wvT = np.ascontiguousarray(np.asarray(L0_attn_Wv, dtype=np.float32).T)
    wrT = np.ascontiguousarray(np.asarray(L0_attn_Wr, dtype=np.float32).T)
    woT = np.ascontiguousarray(np.asarray(L0_attn_Wo, dtype=np.float32).T)
    wg = np.ascontiguousarray(np.asarray(L0_ffn_w_gate, dtype=np.float32))
    wu = np.ascontiguousarray(np.asarray(L0_ffn_w_up, dtype=np.float32))
    wd = np.ascontiguousarray(np.asarray(L0_ffn_w_down, dtype=np.float32))
    td = np.ascontiguousarray(np.asarray(L0_attn_time_decay, dtype=np.float32))
    tmv = np.ascontiguousarray(np.asarray(L0_attn_time_mix_v, dtype=np.float32))
    tmr = np.ascontiguousarray(np.asarray(L0_attn_time_mix_r, dtype=np.float32))
    ident = np.eye(128, dtype=np.float32)

    nc = _get_nc(_debug)
    in_maps = []
    for c in range(NCORES):
        in_maps.append({
            "x": np.ascontiguousarray(x[BPC * c:BPC * (c + 1)]),
            "xT": np.ascontiguousarray(xT[BPC * c:BPC * (c + 1)]),
            "wvT": wvT, "wrT": wrT, "woT": woT,
            "wg": wg, "wu": wu, "wd": wd,
            "td": td, "tmv": tmv, "tmr": tmr, "ident": ident,
        })
    res = run_bass_kernel_spmd(nc, in_maps, core_ids=list(range(NCORES)))
    out = np.concatenate([r["out"] for r in res.results], axis=0)
    if _return_all:
        return out, res.results
    return out
